# version 01 (final)
# baseline (speedup 1.0000x reference)
import numpy as np
import ml_dtypes
import contextlib
import concourse.bass as bass
import concourse.mybir as mybir
from concourse.bass_utils import run_bass_kernel_spmd

F32 = mybir.dt.float32
BF16 = mybir.dt.bfloat16
AF = mybir.ActivationFunctionType
ALU = mybir.AluOpType
AX = mybir.AxisListType
bfdt = ml_dtypes.bfloat16

T = 4096
D = 1024
TT = T // 128
DT = D // 128
TC = T // 512
NCORE = 8


class P:

    def __init__(self):
        self.nc = bass.Bass("TRN2", target_bir_lowering=False)
        self.stack = contextlib.ExitStack()
        self.n = 0
        self.c = 0
        self.sem = self.stack.enter_context(self.nc.semaphore("chain"))
        self.block = self.stack.enter_context(self.nc.Block())

    def _nm(self, base):
        self.n += 1
        return f"{base}_{self.n}"

    def inp(self, name, shape, dtype=F32):
        return self.nc.dram_tensor(name, list(shape), dtype, kind="ExternalInput").ap()

    def out(self, name, shape, dtype=F32):
        return self.nc.dram_tensor(name, list(shape), dtype, kind="ExternalOutput").ap()

    def internal(self, name, shape, dtype=F32):
        return self.nc.dram_tensor(name, list(shape), dtype, kind="Internal").ap()

    def sb(self, shape, dtype=F32, base="sb"):
        return self.stack.enter_context(
            self.nc.sbuf_tensor(self._nm(base), list(shape), dtype))

    def ps(self, shape, dtype=F32, base="ps"):
        return self.stack.enter_context(
            self.nc.psum_tensor(self._nm(base), list(shape), dtype))

    @contextlib.contextmanager
    def scope(self):
        old = self.stack
        with contextlib.ExitStack() as sc:
            self.stack = sc
            yield
        self.stack = old

    def _emit(self, engname, fn, inc, wait=True):
        c0 = self.c if wait else 0
        sem = self.sem

        def body(eng):
            if c0 > 0:
                eng.wait_ge(sem, c0)
            ins = fn(eng)
            if inc:
                ins.then_inc(sem, inc)
        getattr(self.block, engname)(body)
        self.c += inc

    def dma(self, dst, src):
        self._emit("sync", lambda e: e.dma_start(dst, src), 16)

    def dma_acc(self, dst, src):
        self._emit("gpsimd", lambda e: e.dma_start(dst, src, accum_op=ALU.add), 16)

    def mm(self, out, lhsT, rhs, start, stop, inc=None, wait=None):
        if inc is None:
            inc = 1 if stop else 0
        if wait is None:
            wait = start
        self._emit("tensor",
                   lambda e: e.matmul(out, lhsT, rhs, start=start, stop=stop),
                   inc, wait=wait)

    def tr(self, out_ps, in_, eye):
        self._emit("tensor", lambda e: e.transpose(out_ps, in_, eye), 1)

    def act(self, out, in_, func, bias=0.0, scale=1.0):
        self._emit("scalar",
                   lambda e: e.activation(out, in_, func, bias=bias, scale=scale), 1)

    def copy(self, out, in_):
        self._emit("scalar", lambda e: e.copy(out, in_), 1)

    def vcopy(self, out, in_):
        self._emit("vector", lambda e: e.tensor_copy(out, in_), 1)

    def tt(self, out, a, b, op):
        self._emit("vector", lambda e: e.tensor_tensor(out, a, b, op=op), 1)

    def ts(self, out, a, s1, op0, s2=None, op1=None):
        if s2 is None:
            self._emit("vector", lambda e: e.tensor_scalar(out, a, s1, None, op0=op0), 1)
        else:
            self._emit("vector",
                       lambda e: e.tensor_scalar(out, a, s1, s2, op0=op0, op1=op1), 1)

    def stt(self, out, in0, scalar, in1, op0, op1):
        self._emit("vector",
                   lambda e: e.scalar_tensor_tensor(out, in0, scalar, in1,
                                                    op0=op0, op1=op1), 1)

    def red(self, out, in_, op):
        if op == "sum":
            self._emit("vector", lambda e: e.reduce_sum(out, in_, axis=AX.X), 1)
        else:
            self._emit("vector", lambda e: e.reduce_max(out, in_, axis=AX.X), 1)

    def scan(self, out, a, b):
        self._emit("vector",
                   lambda e: e.tensor_tensor_scan(out, a, b, 0.0,
                                                  op0=ALU.mult, op1=ALU.add), 1)

    def recip(self, out, in_):
        self._emit("vector", lambda e: e.reciprocal(out, in_), 1)

    def memset(self, ap, v):
        self._emit("vector", lambda e: e.memset(ap, v), 1)

    def evac(self, out, psum, func, bias, scale=1.0):
        if func == AF.Copy and not isinstance(bias, float):
            if scale != 1.0:
                self.ts(out, psum, scale, ALU.mult, bias, ALU.add)
            else:
                self.ts(out, psum, bias, ALU.add)
        else:
            self.act(out, psum, func, bias=bias, scale=scale)

    def fin(self):
        c = self.c
        sem = self.sem

        def body(eng):
            eng.wait_ge(sem, c)
        self.block.sync(body)

    def close(self):
        self.stack.close()


def load_consts(p):
    cst = {}
    for nm, shape, dt in [("eyeb", (128, 128), BF16), ("eyef", (128, 128), F32),
                          ("ones1b", (1, 128), BF16), ("ones1f", (1, 128), F32),
                          ("onescf", (128, 1), F32), ("onescb", (128, 1), BF16)]:
        ap = p.inp("c_" + nm, shape, dt)
        sbt = p.sb(shape, dt, nm)
        p.dma(sbt[:], ap)
        cst[nm] = sbt
    p.pbc = p.ps((128, 512), F32, "pbc")
    p.ptf = p.ps((128, 128), F32, "ptf")
    p.ptb = p.ps((128, 128), BF16, "ptb")
    p.cst = cst
    return cst


def host_consts():
    d = {}
    d["c_eyeb"] = np.eye(128).astype(bfdt)
    d["c_eyef"] = np.eye(128).astype(np.float32)
    d["c_ones1b"] = np.ones((1, 128), bfdt)
    d["c_ones1f"] = np.ones((1, 128), np.float32)
    d["c_onescf"] = np.ones((128, 1), np.float32)
    d["c_onescb"] = np.ones((128, 1), bfdt)
    return d


def bcast_row(p, dst_sb, src_row, n):
    for c0 in range(0, n, 512):
        w = min(512, n - c0)
        p.mm(p.pbc[0:128, 0:w], p.cst["ones1f"][0:1, 0:128],
             src_row[0:1, c0:c0 + w], True, True)
        p.copy(dst_sb[:, c0:c0 + w], p.pbc[0:128, 0:w])


def load_bias_tile(p, hbm_2d, k, scale=1.0):
    outt = p.sb((128, max(k, 1)), F32, "bias")
    with p.scope():
        stg = p.sb((128, 128), F32, "bstg")
        if k < 128:
            p.memset(stg[:], 0.0)
        p.dma(stg[0:k, 0:128], hbm_2d)
        p.tr(p.ptf[:], stg[:], p.cst["eyef"][:])
        p.act(outt[:], p.ptf[0:128, 0:k], AF.Copy, bias=0.0, scale=scale)
    return outt


def cast_load_lhsT(p, hbm_f32, rows, cols, base="w"):
    tiles = [p.sb((128, cols), BF16, base) for _ in range(rows // 128)]
    cw = min(cols, 2048)
    with p.scope():
        stg = p.sb((128, cw), F32, "wstg")
        for i in range(rows // 128):
            for c0 in range(0, cols, cw):
                p.dma(stg[:, 0:cw], hbm_f32[i * 128:(i + 1) * 128, c0:c0 + cw])
                p.vcopy(tiles[i][:, c0:c0 + cw], stg[:, 0:cw])
    return tiles


def proj_T(p, w_tiles, xT_tiles, out_hbm_bf, dout, func=AF.Copy,
           bias_tile=None, scale=1.0, out_hbm_f32=None, kin=8):
    p0 = p.ps((128, 512), F32, "pp")
    ob = p.sb((128, 512), BF16, "ob")
    of = p.sb((128, 512), F32, "of")
    for mt in range(dout // 128):
        for tc in range(TC):
            for ct in range(kin):
                p.mm(p0[:], w_tiles[ct][:, mt * 128:(mt + 1) * 128],
                     xT_tiles[ct][:, tc * 512:(tc + 1) * 512],
                     ct == 0, ct == kin - 1)
            b = bias_tile[:, mt:mt + 1] if bias_tile is not None else 0.0
            if out_hbm_f32 is not None:
                p.evac(of[:], p0[:], func, b, scale)
                p.dma(out_hbm_f32[mt * 128:(mt + 1) * 128, tc * 512:(tc + 1) * 512],
                      of[:])
                if out_hbm_bf is not None:
                    p.vcopy(ob[:], of[:])
                    p.dma(out_hbm_bf[mt * 128:(mt + 1) * 128,
                                     tc * 512:(tc + 1) * 512], ob[:])
            else:
                p.evac(ob[:], p0[:], func, b, scale)
                p.dma(out_hbm_bf[mt * 128:(mt + 1) * 128,
                                 tc * 512:(tc + 1) * 512], ob[:])


def ln_T(p, src_hbm_f32, dst_bf, dst_f32=None):
    with p.scope():
        xt = [p.sb((128, T), F32, "lnx") for _ in range(DT)]
        for dt in range(DT):
            p.dma(xt[dt][:], src_hbm_f32[dt * 128:(dt + 1) * 128, :])
        sq = [p.sb((128, 512), F32, "lnsq") for _ in range(DT)]
        pm = p.ps((128, 512), F32, "lnpm")
        pv = p.ps((128, 512), F32, "lnpv")
        mu = p.sb((1, T), F32, "lnmu")
        rs = p.sb((1, T), F32, "lnrs")
        ones = p.cst["onescf"]
        for tc in range(TC):
            s = slice(tc * 512, (tc + 1) * 512)
            for dt in range(DT):
                p.mm(pm[0:1, 0:512], ones[0:128, 0:1], xt[dt][:, s],
                     dt == 0, dt == DT - 1)
            for dt in range(DT):
                p.act(sq[dt][:], xt[dt][:, s], AF.Square)
            for dt in range(DT):
                p.mm(pv[0:1, 0:512], ones[0:128, 0:1], sq[dt][:],
                     dt == 0, dt == DT - 1)
            p.ts(mu[0:1, s], pm[0:1, 0:512], 1.0 / D, ALU.mult)
            p.ts(rs[0:1, s], pv[0:1, 0:512], 1.0 / D, ALU.mult)
        musq = p.sb((1, T), F32, "lnms")
        p.tt(musq[:], mu[:], mu[:], ALU.mult)
        p.tt(rs[:], rs[:], musq[:], ALU.subtract)
        p.ts(rs[:], rs[:], 1e-5, ALU.add)
        p.act(rs[:], rs[:], AF.Sqrt)
        p.recip(rs[:], rs[:])
        mB = p.sb((128, 512), F32, "lnmB")
        rB = p.sb((128, 512), F32, "lnrB")
        tmp = p.sb((128, 512), F32, "lntmp")
        ob = p.sb((128, 512), BF16, "lnob")
        for tc in range(TC):
            s = slice(tc * 512, (tc + 1) * 512)
            bcast_row(p, mB, mu[0:1, s], 512)
            bcast_row(p, rB, rs[0:1, s], 512)
            for dt in range(DT):
                p.tt(tmp[:], xt[dt][:, s], mB[:, 0:512], ALU.subtract)
                p.tt(tmp[:], tmp[:], rB[:, 0:512], ALU.mult)
                if dst_f32 is not None:
                    p.dma(dst_f32[dt * 128:(dt + 1) * 128, s], tmp[:])
                p.vcopy(ob[:], tmp[:])
                p.dma(dst_bf[dt * 128:(dt + 1) * 128, s], ob[:])


def attn_1head(p, qT_hbm, kT_hbm, vext_hbm, ctx_hbm_bf, causal, bm_sb=None):
    with p.scope():
        KTs = [p.sb((128, T), BF16, "akt") for _ in range(DT)]
        Vx = [p.sb((128, 1040), BF16, "avx") for _ in range(TT)]
        for dt in range(DT):
            p.dma(KTs[dt][:], kT_hbm[dt * 128:(dt + 1) * 128, :])
        for kt in range(TT):
            p.dma(Vx[kt][:], vext_hbm[kt * 128:(kt + 1) * 128, :])
        QTq = [p.sb((128, 512), BF16, "aqt") for _ in range(DT)]
        Pt = [p.sb((128, 512), BF16, "apt") for _ in range(TT)]
        ps0 = p.ps((128, 512), F32, "aps0")
        psc = [p.ps((128, 512), F32, "apsc") for _ in range(4)]
        srow = p.sb((1, 512), F32, "asr")
        rsB = p.sb((128, 512), F32, "arsB")
        ob = p.sb((128, 512), BF16, "aob")
        for qc in range(TC):
            nkt = min(TT, (qc * 512 + 511) // 128 + 1) if causal else TT
            for dt in range(DT):
                p.dma(QTq[dt][:], qT_hbm[dt * 128:(dt + 1) * 128,
                                         qc * 512:(qc + 1) * 512])
            for kt in range(nkt):
                for ct in range(DT):
                    p.mm(ps0[:], KTs[ct][:, kt * 128:(kt + 1) * 128], QTq[ct][:],
                         ct == 0, ct == DT - 1)
                p.act(Pt[kt][:], ps0[:], AF.Exp)
                if causal and kt * 128 + 127 > qc * 512:
                    off = qc * 512 - kt * 128 + 640
                    p.tt(Pt[kt][:], Pt[kt][:], bm_sb[:, off:off + 512], ALU.mult)
            for kt in range(nkt):
                p.mm(ps0[0:1, 0:512], Vx[kt][:, 1024:1025], Pt[kt][:],
                     kt == 0, kt == nkt - 1)
            p.recip(srow[:], ps0[0:1, 0:512])
            bcast_row(p, rsB, srow, 512)
            for mg in range(2):
                for kt in range(nkt):
                    for mi in range(4):
                        mt = mg * 4 + mi
                        p.mm(psc[mi][:], Vx[kt][:, mt * 128:(mt + 1) * 128],
                             Pt[kt][:], kt == 0, kt == nkt - 1)
                for mi in range(4):
                    mt = mg * 4 + mi
                    p.tt(ob[:], psc[mi][:], rsB[:, 0:512], ALU.mult)
                    p.dma(ctx_hbm_bf[mt * 128:(mt + 1) * 128,
                                     qc * 512:(qc + 1) * 512], ob[:])


def attn_16head(p, qT_hbm, kT_hbm, vext_hbm, ctx_hbm_bf, causal,
                bm_sb=None, alibi_ct=None, rope=None):
    with p.scope():
        KTs = [p.sb((128, T), BF16, "bkt") for _ in range(DT)]
        Vx = [p.sb((128, 1040), BF16, "bvx") for _ in range(TT)]
        for dt in range(DT):
            p.dma(KTs[dt][:], kT_hbm[dt * 128:(dt + 1) * 128, :])
        for kt in range(TT):
            p.dma(Vx[kt][:], vext_hbm[kt * 128:(kt + 1) * 128, :])
        Qh = p.sb((128, 512), BF16, "bqh")
        if rope is not None:
            qlo_s = p.sb((16, 512), BF16, "bqlo")
            qhi_s = p.sb((16, 512), BF16, "bqhi")
        Pt = [p.sb((128, 512), BF16, "bpt") for _ in range(TT)]
        ps0 = p.ps((128, 512), F32, "bps0")
        psc = p.ps((65, 512), F32, "bpsc")
        srow = p.sb((1, 512), F32, "bsr")
        rsB = p.sb((64, 512), F32, "brsB")
        ob = p.sb((64, 512), BF16, "bob")
        for h in range(16):
            dt = h // 2
            o = (h % 2) * 64
            for qc in range(TC):
                nkt = min(TT, (qc * 512 + 511) // 128 + 1) if causal else TT
                p.dma(Qh[o:o + 64, :], qT_hbm[dt * 128 + o:dt * 128 + o + 64,
                                              qc * 512:(qc + 1) * 512])
                if rope is not None:
                    p.dma(qlo_s[:], rope["qlo_h"][0:16, qc * 512:(qc + 1) * 512])
                    p.dma(qhi_s[:], rope["qhi_h"][0:16, qc * 512:(qc + 1) * 512])
                for kt in range(nkt):
                    last = rope is None
                    p.mm(ps0[:], KTs[dt][o:o + 64, kt * 128:(kt + 1) * 128],
                         Qh[o:o + 64, :], True, last)
                    if rope is not None:
                        p.mm(ps0[:], rope["klo"][0:16, kt * 128:(kt + 1) * 128],
                             qlo_s[:], False, False)
                        p.mm(ps0[:], rope["khi"][0:16, kt * 128:(kt + 1) * 128],
                             qhi_s[:], False, True)
                    if alibi_ct is not None:
                        p.act(Pt[kt][:], ps0[:], AF.Exp,
                              bias=alibi_ct[h][:, kt:kt + 1])
                    else:
                        p.act(Pt[kt][:], ps0[:], AF.Exp)
                    if causal and kt * 128 + 127 > qc * 512:
                        off = qc * 512 - kt * 128 + 640
                        p.tt(Pt[kt][:], Pt[kt][:], bm_sb[:, off:off + 512],
                             ALU.mult)
                for kt in range(nkt):
                    p.mm(psc[0:65, 0:512], Vx[kt][:, h * 65:(h + 1) * 65],
                         Pt[kt][:], kt == 0, kt == nkt - 1)
                p.recip(srow[:], psc[64:65, 0:512])
                p.mm(p.pbc[0:64, 0:512], p.cst["ones1f"][0:1, 0:64],
                     srow[0:1, 0:512], True, True)
                p.copy(rsB[:], p.pbc[0:64, 0:512])
                p.tt(ob[:], psc[0:64, 0:512], rsB[:], ALU.mult)
                p.dma(ctx_hbm_bf[dt * 128 + o:dt * 128 + o + 64,
                                 qc * 512:(qc + 1) * 512], ob[:])


def build_vext_from_vT(p, vT_hbm_bf, vext_hbm, heads):
    with p.scope():
        vt = p.sb((128, 128), BF16, "vvt")
        row = p.sb((128, 1040), BF16, "vrow")
        for tt in range(TT):
            p.memset(row[:], 0.0)
            for dt in range(DT):
                p.dma(vt[:], vT_hbm_bf[dt * 128:(dt + 1) * 128,
                                       tt * 128:(tt + 1) * 128])
                p.tr(p.ptb[:], vt[:], p.cst["eyeb"][:])
                if heads == 1:
                    p.vcopy(row[:, dt * 128:(dt + 1) * 128], p.ptb[:])
                else:
                    h0 = dt * 2
                    p.vcopy(row[:, h0 * 65:h0 * 65 + 64], p.ptb[0:128, 0:64])
                    p.vcopy(row[:, (h0 + 1) * 65:(h0 + 1) * 65 + 64],
                            p.ptb[0:128, 64:128])
            if heads == 1:
                p.memset(row[:, 1024:1025], 1.0)
            else:
                for h in range(16):
                    p.memset(row[:, h * 65 + 64:h * 65 + 65], 1.0)
            p.dma(vext_hbm[tt * 128:(tt + 1) * 128, :], row[:])


def moe_expert_dense(p, W1_hbm, W2_hbm, b1_hbm2d, b2_hbm2d, xnT_hbm_bf,
                     yacc_hbm, wT_hbm, e_idx, first, func):
    with p.scope():
        W1 = cast_load_lhsT(p, W1_hbm, 1024, 4096, "mw1")
        W2 = cast_load_lhsT(p, W2_hbm, 4096, 1024, "mw2")
        b1t = load_bias_tile(p, b1_hbm2d, 32)
        b2t = load_bias_tile(p, b2_hbm2d, 8)
        xs = [p.sb((128, 512), BF16, "mxs") for _ in range(DT)]
        hs = [p.sb((128, 512), BF16, "mhs") for _ in range(32)]
        weB = p.sb((128, 512), F32, "mweB")
        wrow = p.sb((1, 512), F32, "mwrow")
        yo = p.sb((128, 512), F32, "myo")
        ph = p.ps((128, 512), F32, "mph")
        py = [p.ps((128, 512), F32, "mpy") for _ in range(4)]
        for tc in range(TC):
            s = slice(tc * 512, (tc + 1) * 512)
            for dt in range(DT):
                p.dma(xs[dt][:], xnT_hbm_bf[dt * 128:(dt + 1) * 128, s])
            if wT_hbm is not None:
                p.dma(wrow[:], wT_hbm[e_idx:e_idx + 1, s])
                p.mm(p.pbc[0:128, 0:512], p.cst["ones1f"][0:1, 0:128],
                     wrow[0:1, 0:512], True, True)
                p.copy(weB[:], p.pbc[0:128, 0:512])
            for ht in range(32):
                for ct in range(DT):
                    p.mm(ph[:], W1[ct][:, ht * 128:(ht + 1) * 128], xs[ct][:],
                         ct == 0, ct == DT - 1)
                p.act(hs[ht][:], ph[:], func, bias=b1t[:, ht:ht + 1])
            for mg in range(2):
                for ht in range(32):
                    for mi in range(4):
                        mt = mg * 4 + mi
                        p.mm(py[mi][:], W2[ht][:, mt * 128:(mt + 1) * 128],
                             hs[ht][:], ht == 0, ht == 31)
                for mi in range(4):
                    mt = mg * 4 + mi
                    if wT_hbm is not None:
                        p.stt(yo[:], py[mi][:], b2t[:, mt:mt + 1], weB[:],
                              ALU.add, ALU.mult)
                    else:
                        p.ts(yo[:], py[mi][:], b2t[:, mt:mt + 1], ALU.add)
                    if first:
                        p.dma(yacc_hbm[mt * 128:(mt + 1) * 128, s], yo[:])
                    else:
                        p.dma_acc(yacc_hbm[mt * 128:(mt + 1) * 128, s], yo[:])


def fff(p, xnT_bf_hbm, xnT_f32_hbm, nwT_hbm, nb_hbm, W1cat_hbm, b1cat_hbm,
        W2cat_hbm, b2_hbm, outT_bf_hbm, func, iotaB_sb, cvh_sb, cvl_sb,
        out_scale=1.0, outT_f32_hbm=None, dbg_leaf=None, dbg_L=None):
    with p.scope():
        xT = [p.sb((128, T), BF16, "fx") for _ in range(DT)]
        for dt in range(DT):
            p.dma(xT[dt][:], xnT_bf_hbm[dt * 128:(dt + 1) * 128, :])
        b1t = load_bias_tile(p, b1cat_hbm, 32)
        b2bf = cast_load_lhsT(p, b2_hbm, 256, 1024, "fb2")
        ph = p.ps((128, 512), F32, "fph")
        leafrow = p.sb((1, T), F32, "flr")
        rsc = p.scope()
        rsc.__enter__()
        nwT = [p.sb((128, 256), F32, "fnw") for _ in range(DT)]
        for ct in range(DT):
            p.dma(nwT[ct][:], nwT_hbm[ct * 128:(ct + 1) * 128, :])
        xf = [p.sb((128, 128), F32, "fxf") for _ in range(DT)]
        nbr = p.sb((1, 255), F32, "fnbr")
        p.dma(nbr[:], nb_hbm)
        nbB = p.sb((128, 255), F32, "fnbB")
        bcast_row(p, nbB, nbr, 255)
        Lsb = p.sb((128, 255), F32, "fL")
        node = p.sb((128, 1), F32, "fnode")
        eq = p.sb((128, 128), F32, "feq")
        logit = p.sb((128, 1), F32, "flog")
        bit = p.sb((128, 1), F32, "fbit")
        for tt in range(TT):
            for ct in range(DT):
                p.dma(xf[ct][:], xnT_f32_hbm[ct * 128:(ct + 1) * 128,
                                             tt * 128:(tt + 1) * 128])
            for ct in range(DT):
                p.mm(ph[0:128, 0:255], xf[ct][:, 0:128],
                     nwT[ct][:, 0:255], ct == 0, ct == DT - 1)
            p.tt(Lsb[:], ph[0:128, 0:255], nbB[:], ALU.add)
            if dbg_L is not None and tt == 0:
                p.dma(dbg_L, Lsb[:])
            p.memset(node[:], 0.0)
            for d in range(8):
                n0 = (1 << d) - 1
                w = 1 << d
                p.ts(eq[0:128, 0:w], iotaB_sb[:, n0:n0 + w], node[:], ALU.is_equal)
                p.tt(eq[0:128, 0:w], eq[0:128, 0:w], Lsb[:, n0:n0 + w], ALU.mult)
                p.red(logit[:], eq[0:128, 0:w], "sum")
                p.ts(bit[:], logit[:], 0.0, ALU.is_gt)
                p.ts(node[:], node[:], 2.0, ALU.mult, 1.0, ALU.add)
                p.tt(node[:], node[:], bit[:], ALU.add)
            p.ts(node[:], node[:], 255.0, ALU.subtract)
            p.tr(p.ptf[0:1, 0:128], node[0:128, 0:1], p.cst["eyef"][:])
            p.copy(leafrow[0:1, tt * 128:(tt + 1) * 128], p.ptf[0:1, 0:128])
        if dbg_leaf is not None:
            p.dma(dbg_leaf, leafrow[:])
        rsc.__exit__(None, None, None)
        W1 = cast_load_lhsT(p, W1cat_hbm, 1024, 4096, "fw1")
        hs = [p.sb((128, 512), BF16, "fhs") for _ in range(32)]
        lb = p.sb((128, 512), F32, "flb")
        msk = p.sb((128, 512), BF16, "fmsk")
        oh = p.sb((128, 512), BF16, "foh")
        w2stg = p.sb((128, 512), F32, "fw2s")
        w2t = p.sb((128, 1024), BF16, "fw2")
        yo = p.sb((128, 512), F32, "fyo")
        ob = p.sb((128, 512), BF16, "fob")
        py = [p.ps((128, 512), F32, "fpy") for _ in range(4)]
        for tc in range(TC):
            s = slice(tc * 512, (tc + 1) * 512)
            bcast_row(p, lb, leafrow[0:1, s], 512)
            for ht in range(32):
                for ct in range(DT):
                    p.mm(ph[:], W1[ct][:, ht * 128:(ht + 1) * 128],
                         xT[ct][:, s], ct == 0, ct == DT - 1)
                p.act(hs[ht][:], ph[:], func, bias=b1t[:, ht:ht + 1])
                p.ts(msk[:], lb[:, 0:512], cvh_sb[:, ht:ht + 1], ALU.is_equal)
                p.tt(hs[ht][:], hs[ht][:], msk[:], ALU.mult)
            for mg in range(2):
                for ht in range(32):
                    for c2 in range(2):
                        p.dma(w2stg[:], W2cat_hbm[ht * 128:(ht + 1) * 128,
                                                  c2 * 512:(c2 + 1) * 512])
                        p.vcopy(w2t[:, c2 * 512:(c2 + 1) * 512], w2stg[:])
                    for mi in range(4):
                        mt = mg * 4 + mi
                        p.mm(py[mi][:], w2t[:, mt * 128:(mt + 1) * 128],
                             hs[ht][:], ht == 0, False,
                             inc=(1 if mi == 3 else 0), wait=(mi == 0))
                for lt in range(2):
                    p.ts(oh[:], lb[:, 0:512], cvl_sb[:, lt:lt + 1], ALU.is_equal)
                    for mi in range(4):
                        mt = mg * 4 + mi
                        p.mm(py[mi][:], b2bf[lt][:, mt * 128:(mt + 1) * 128],
                             oh[:], False, lt == 1,
                             inc=(1 if mi == 3 else None), wait=(mi == 0))
                for mi in range(4):
                    mt = mg * 4 + mi
                    if outT_f32_hbm is not None:
                        p.act(yo[:], py[mi][:], AF.Copy, scale=out_scale)
                        p.dma(outT_f32_hbm[mt * 128:(mt + 1) * 128, s], yo[:])
                        if outT_bf_hbm is not None:
                            p.vcopy(ob[:], yo[:])
                            p.dma(outT_bf_hbm[mt * 128:(mt + 1) * 128, s], ob[:])
                    else:
                        p.act(ob[:], py[mi][:], AF.Copy, scale=out_scale)
                        p.dma(outT_bf_hbm[mt * 128:(mt + 1) * 128, s], ob[:])


def launch_prep():
    p = P()
    load_consts(p)
    x = p.inp("x", (T, D))
    xT = p.out("xT", (D, T))
    xn0T = p.out("xn0T", (D, T), BF16)
    xnfT = p.out("xnfT", (D, T))
    xnfTb = p.out("xnfTb", (D, T), BF16)
    xr = p.sb((128, D), F32, "xr")
    xn = p.sb((128, D), F32, "xn")
    xnf = p.sb((128, D), F32, "xnf")
    sq = p.sb((128, D), F32, "sq")
    mu = p.sb((128, 1), F32, "mu")
    var = p.sb((128, 1), F32, "var")
    r1 = p.sb((128, 1), F32, "r1")
    mu2 = p.sb((128, 1), F32, "mu2")
    t0 = p.sb((128, 1), F32, "t0")
    r2 = p.sb((128, 1), F32, "r2")
    cb = p.sb((128, 128), BF16, "cbuf")
    cf = p.sb((128, 128), F32, "cfbuf")
    for tt in range(TT):
        srow = slice(tt * 128, (tt + 1) * 128)
        p.dma(xr[:], x[srow, :])
        p.red(mu[:], xr[:], "sum")
        p.ts(mu[:], mu[:], 1.0 / D, ALU.mult)
        p.act(sq[:], xr[:], AF.Square)
        p.red(var[:], sq[:], "sum")
        p.ts(var[:], var[:], 1.0 / D, ALU.mult)
        p.tt(t0[:], mu[:], mu[:], ALU.mult)
        p.tt(var[:], var[:], t0[:], ALU.subtract)
        p.ts(var[:], var[:], 1e-5, ALU.add)
        p.act(var[:], var[:], AF.Sqrt)
        p.recip(r1[:], var[:])
        p.ts(xn[:], xr[:], mu[:], ALU.subtract)
        p.ts(xn[:], xn[:], r1[:], ALU.mult)
        p.red(mu2[:], xn[:], "sum")
        p.ts(mu2[:], mu2[:], 1.0 / D, ALU.mult)
        p.act(sq[:], xn[:], AF.Square)
        p.red(var[:], sq[:], "sum")
        p.ts(var[:], var[:], 1.0 / (D - 1), ALU.mult)
        p.tt(t0[:], mu2[:], mu2[:], ALU.mult)
        p.ts(t0[:], t0[:], D / (D - 1.0), ALU.mult)
        p.tt(var[:], var[:], t0[:], ALU.subtract)
        p.act(var[:], var[:], AF.Sqrt)
        p.ts(var[:], var[:], 1e-5, ALU.add)
        p.act(var[:], var[:], AF.Sqrt)
        p.recip(r2[:], var[:])
        p.ts(xnf[:], xn[:], mu2[:], ALU.subtract)
        p.ts(xnf[:], xnf[:], r2[:], ALU.mult)
        for dt in range(DT):
            sd = slice(dt * 128, (dt + 1) * 128)
            p.tr(p.ptf[:], xr[:, sd], p.cst["eyef"][:])
            p.copy(cf[:], p.ptf[:])
            p.dma(xT[sd, srow], cf[:])
            p.vcopy(cb[:], xn[:, sd])
            p.tr(p.ptb[:], cb[:], p.cst["eyeb"][:])
            p.vcopy(cb[:], p.ptb[:])
            p.dma(xn0T[sd, srow], cb[:])
            p.tr(p.ptf[:], xnf[:, sd], p.cst["eyef"][:])
            p.copy(cf[:], p.ptf[:])
            p.dma(xnfT[sd, srow], cf[:])
            p.vcopy(cb[:], cf[:])
            p.dma(xnfTb[sd, srow], cb[:])
    p.fin()
    return p


def launch_l0attn():
    p = P()
    load_consts(p)
    xn0T = p.inp("xn0T", (D, T), BF16)
    Wq = p.inp("Wq", (D, D)); bq = p.inp("bq", (8, 128))
    Wk = p.inp("Wk", (D, D)); bk = p.inp("bk", (8, 128))
    Wv = p.inp("Wv", (D, D)); bv = p.inp("bv", (8, 128))
    Wo = p.inp("Wo", (D, D)); bo = p.inp("bo", (8, 128))
    bm = p.inp("c_bm", (128, 1280), BF16)
    s0T = p.out("s0T", (D, T))
    qT = p.internal("qT_i", (D, T), BF16)
    kT = p.internal("kT_i", (D, T), BF16)
    vT = p.internal("vT_i", (D, T), BF16)
    vext = p.internal("vext_i", (T, 1040), BF16)
    ctxT = p.internal("ctxT_i", (D, T), BF16)
    bm_sb = p.sb((128, 1280), BF16, "bm")
    p.dma(bm_sb[:], bm)
    with p.scope():
        xs = [p.sb((128, T), BF16, "xs") for _ in range(DT)]
        for dt in range(DT):
            p.dma(xs[dt][:], xn0T[dt * 128:(dt + 1) * 128, :])
        for (W, b, outT, scale) in [(Wq, bq, qT, 1.0 / 32.0), (Wk, bk, kT, 1.0),
                                    (Wv, bv, vT, 1.0)]:
            with p.scope():
                wt = cast_load_lhsT(p, W, D, D, "w")
                bt = load_bias_tile(p, b, 8, scale=scale)
                proj_T(p, wt, xs, outT, D, func=AF.Relu, bias_tile=bt, scale=scale)
    build_vext_from_vT(p, vT, vext, 1)
    attn_1head(p, qT, kT, vext, ctxT, causal=True, bm_sb=bm_sb)
    with p.scope():
        xs = [p.sb((128, T), BF16, "xs2") for _ in range(DT)]
        for dt in range(DT):
            p.dma(xs[dt][:], ctxT[dt * 128:(dt + 1) * 128, :])
        wt = cast_load_lhsT(p, Wo, D, D, "wo")
        bt = load_bias_tile(p, bo, 8)
        proj_T(p, wt, xs, None, D, func=AF.Relu, bias_tile=bt, out_hbm_f32=s0T)
    p.fin()
    return p


def launch_fff_qkv(which):
    p = P()
    load_consts(p)
    xnfTb = p.inp("xnfTb", (D, T), BF16)
    xnfTf = p.inp("xnfTf", (D, T))
    nwT = p.inp("nwT", (D, 256))
    nb = p.inp("nb", (1, 255))
    W1c = p.inp("W1c", (D, 4096))
    b1c = p.inp("b1c", (32, 128))
    W2c = p.inp("W2c", (4096, D))
    b2 = p.inp("b2", (256, D))
    iotaB = p.inp("c_iotaB", (128, 256))
    cvh = p.inp("c_cvh", (128, 32))
    cvl = p.inp("c_cvl", (128, 2))
    outT = p.out("fT", (D, T), BF16)
    outTf = None
    iotaB_sb = p.sb((128, 256), F32, "iB")
    cvh_sb = p.sb((128, 32), F32, "cvh")
    cvl_sb = p.sb((128, 2), F32, "cvl")
    p.dma(iotaB_sb[:], iotaB)
    p.dma(cvh_sb[:], cvh)
    p.dma(cvl_sb[:], cvl)
    scale = 1.0 / 32.0 if which == "q" else 1.0
    fff(p, xnfTb, xnfTf, nwT, nb, W1c, b1c, W2c, b2, outT, AF.Relu,
        iotaB_sb, cvh_sb, cvl_sb, out_scale=scale, outT_f32_hbm=outTf)
    p.fin()
    return p


def launch_l0ffn_attn():
    p = P()
    load_consts(p)
    qT = p.inp("qT", (D, T), BF16)
    kT = p.inp("kT", (D, T), BF16)
    vT = p.inp("vT", (D, T), BF16)
    xnfT = p.inp("xnfT", (D, T))
    yT = p.out("yT", (D, T))
    yTb = p.out("yTb", (D, T), BF16)
    vext = p.internal("vext_i", (T, 1040), BF16)
    ctxT = p.internal("ctxT_i", (D, T), BF16)
    build_vext_from_vT(p, vT, vext, 1)
    attn_1head(p, qT, kT, vext, ctxT, causal=False)
    with p.scope():
        cx = p.sb((128, T), BF16, "cx")
        xf = p.sb((128, T), F32, "xf")
        yf = p.sb((128, T), F32, "yf")
        yb = p.sb((128, T), BF16, "yb")
        for dt in range(DT):
            sd = slice(dt * 128, (dt + 1) * 128)
            p.dma(cx[:], ctxT[sd, :])
            p.dma(xf[:], xnfT[sd, :])
            p.tt(yf[:], xf[:], cx[:], ALU.add)
            p.dma(yT[sd, :], yf[:])
            p.vcopy(yb[:], yf[:])
            p.dma(yTb[sd, :], yb[:])
    p.fin()
    return p


def launch_l0ffn_final():
    p = P()
    load_consts(p)
    yTb = p.inp("yTb", (D, T), BF16)
    yT = p.inp("yT", (D, T))
    xT = p.inp("xT", (D, T))
    s0T = p.inp("s0T", (D, T))
    nwT = p.inp("nwT", (D, 256))
    nb = p.inp("nb", (1, 255))
    W1c = p.inp("W1c", (D, 4096))
    b1c = p.inp("b1c", (32, 128))
    W2c = p.inp("W2c", (4096, D))
    b2 = p.inp("b2", (256, D))
    iotaB = p.inp("c_iotaB", (128, 256))
    cvh = p.inp("c_cvh", (128, 32))
    cvl = p.inp("c_cvl", (128, 2))
    x1T = p.out("x1T", (D, T))
    fTf = p.internal("fTf_i", (D, T))
    iotaB_sb = p.sb((128, 256), F32, "iB")
    cvh_sb = p.sb((128, 32), F32, "cvh")
    cvl_sb = p.sb((128, 2), F32, "cvl")
    p.dma(iotaB_sb[:], iotaB)
    p.dma(cvh_sb[:], cvh)
    p.dma(cvl_sb[:], cvl)
    fff(p, yTb, yT, nwT, nb, W1c, b1c, W2c, b2, None, AF.Gelu,
        iotaB_sb, cvh_sb, cvl_sb, outT_f32_hbm=fTf)
    with p.scope():
        a = p.sb((128, T), F32, "ca")
        b_ = p.sb((128, T), F32, "cb2")
        g = p.sb((128, T), F32, "cg")
        for dt in range(DT):
            sd = slice(dt * 128, (dt + 1) * 128)
            p.dma(a[:], yT[sd, :])
            p.dma(b_[:], fTf[sd, :])
            p.tt(g[:], a[:], b_[:], ALU.add)
            p.act(g[:], g[:], AF.Silu)
            p.dma(a[:], s0T[sd, :])
            p.tt(g[:], g[:], a[:], ALU.mult)
            p.dma(a[:], xT[sd, :])
            p.tt(g[:], g[:], a[:], ALU.add)
            p.dma(x1T[sd, :], g[:])
    p.fin()
    return p


def launch_ln(also_f32):
    p = P()
    load_consts(p)
    src = p.inp("lnin", (D, T))
    dst = p.out("lnout", (D, T), BF16)
    dstf = p.out("lnoutf", (D, T)) if also_f32 else None
    ln_T(p, src, dst, dstf)
    p.fin()
    return p


def launch_l1attn():
    p = P()
    load_consts(p)
    xnT = p.inp("xnT", (D, T), BF16)
    Wq = p.inp("Wq", (D, D)); bq = p.inp("bq", (8, 128))
    Wk = p.inp("Wk", (D, D)); bk = p.inp("bk", (8, 128))
    Wv = p.inp("Wv", (D, D)); bv = p.inp("bv", (8, 128))
    Wo = p.inp("Wo", (D, D)); bo = p.inp("bo", (8, 128))
    ct0 = p.inp("c_ct0", (128, 32))
    slopes = p.inp("c_slopes", (1, 16))
    s1T = p.out("s1T", (D, T))
    qT = p.internal("qT_i", (D, T), BF16)
    kT = p.internal("kT_i", (D, T), BF16)
    vT = p.internal("vT_i", (D, T), BF16)
    vext = p.internal("vext_i", (T, 1040), BF16)
    ctxT = p.internal("ctxT_i", (D, T), BF16)
    with p.scope():
        xs = [p.sb((128, T), BF16, "xs") for _ in range(DT)]
        for dt in range(DT):
            p.dma(xs[dt][:], xnT[dt * 128:(dt + 1) * 128, :])
        for (W, b, outT, scale) in [(Wq, bq, qT, 0.125), (Wk, bk, kT, 1.0),
                                    (Wv, bv, vT, 1.0)]:
            with p.scope():
                wt = cast_load_lhsT(p, W, D, D, "w")
                bt = load_bias_tile(p, b, 8, scale=scale)
                proj_T(p, wt, xs, outT, D, func=AF.Copy, bias_tile=bt, scale=scale)
    build_vext_from_vT(p, vT, vext, 16)
    sl = p.sb((1, 16), F32, "sl")
    p.dma(sl[:], slopes)
    slB = p.sb((128, 16), F32, "slB")
    bcast_row(p, slB, sl, 16)
    ct0s = p.sb((128, 32), F32, "ct0")
    p.dma(ct0s[:], ct0)
    alibi = []
    for h in range(16):
        a = p.sb((128, 32), F32, "al")
        p.ts(a[:], ct0s[:], slB[:, h:h + 1], ALU.mult)
        alibi.append(a)
    attn_16head(p, qT, kT, vext, ctxT, causal=False, alibi_ct=alibi)
    with p.scope():
        xs = [p.sb((128, T), BF16, "xs2") for _ in range(DT)]
        for dt in range(DT):
            p.dma(xs[dt][:], ctxT[dt * 128:(dt + 1) * 128, :])
        wt = cast_load_lhsT(p, Wo, D, D, "wo")
        bt = load_bias_tile(p, bo, 8)
        proj_T(p, wt, xs, None, D, func=AF.Copy, bias_tile=bt, out_hbm_f32=s1T)
    p.fin()
    return p


def _softmax_free(p, probs, psum_ap, scr):
    p.red(scr["m"][:], probs[:], "max")
    p.ts(scr["m"][:], scr["m"][:], -1.0, ALU.mult)
    p.ts(probs[:], probs[:], scr["m"][:], ALU.add)
    p.act(probs[:], probs[:], AF.Exp)
    p.red(scr["s"][:], probs[:], "sum")
    p.recip(scr["s"][:], scr["s"][:])
    p.ts(probs[:], probs[:], scr["s"][:], ALU.mult)


def _onehot_max(p, probs, oh, v, scr):
    p.red(v[:], probs[:], "max")
    p.ts(oh[:], probs[:], v[:], ALU.is_ge)
    p.scan(scr["cum"][:], scr["ones8"][:], oh[:])
    p.ts(scr["cum"][:], scr["cum"][:], 1.0, ALU.is_le)
    p.tt(oh[:], oh[:], scr["cum"][:], ALU.mult)


def launch_route(topk, with_bias):
    p = P()
    load_consts(p)
    xnT = p.inp("xnTf", (D, T))
    Wr = p.inp("Wr", (D, 8))
    brsw = p.inp("brsw", (1, 8)) if with_bias else None
    tri128 = p.inp("c_tri128", (128, 128), BF16)
    ohs = [p.out(f"oh{k+1}", (T, 8)) for k in range(topk)]
    wvs = [p.out(f"w{k+1}v", (T, 1)) for k in range(topk)]
    rvs = [p.out(f"r{k+1}v", (T, 1)) for k in range(topk)]
    coreto = p.out("coretot", (topk, 8))
    xs = [p.sb((128, 128), F32, "xs") for _ in range(DT)]
    wr = [p.sb((128, 8), F32, "wr") for _ in range(DT)]
    for ct in range(DT):
        p.dma(wr[ct][:], Wr[ct * 128:(ct + 1) * 128, :])
    tri = p.sb((128, 128), BF16, "tri")
    p.dma(tri[:], tri128)
    pc = p.ps((128, 8), F32, "pc")
    pt = p.ps((1, 8), F32, "pt")
    probs = p.sb((128, 8), F32, "pr")
    p2 = p.sb((128, 8), F32, "p2")
    oh1 = p.sb((128, 8), F32, "oh1")
    oh2 = p.sb((128, 8), F32, "oh2")
    ohb = p.sb((128, 8), BF16, "ohb")
    v1 = p.sb((128, 1), F32, "v1")
    v2 = p.sb((128, 1), F32, "v2")
    s12 = p.sb((128, 1), F32, "s12")
    rk = p.sb((128, 8), F32, "rk")
    r_ = p.sb((128, 1), F32, "r")
    off = p.sb((128, 1), F32, "off")
    runoff = [p.sb((1, 8), F32, "run") for _ in range(topk)]
    runB = p.sb((128, 8), F32, "runB")
    bswB = p.sb((128, 8), F32, "bswB")
    scr = {"ones8": p.sb((128, 8), F32, "on8"), "cum": p.sb((128, 8), F32, "cum"),
           "m": p.sb((128, 1), F32, "srm"), "s": p.sb((128, 1), F32, "srs")}
    p.memset(scr["ones8"][:], 1.0)
    for k in range(topk):
        p.memset(runoff[k][:], 0.0)
    if with_bias:
        bsr = p.sb((1, 8), F32, "bsr")
        p.dma(bsr[:], brsw)
        bcast_row(p, bswB, bsr, 8)
    for tt in range(TT):
        st = slice(tt * 128, (tt + 1) * 128)
        for ct in range(DT):
            p.dma(xs[ct][:], xnT[ct * 128:(ct + 1) * 128, st])
        for ct in range(DT):
            p.mm(pc[0:128, 0:8], xs[ct][:], wr[ct][:, 0:8], ct == 0, ct == DT - 1)
        if with_bias:
            p.tt(probs[:], pc[0:128, 0:8], bswB[:], ALU.add)
        else:
            p.copy(probs[:], pc[0:128, 0:8])
        _softmax_free(p, probs, None, scr)
        _onehot_max(p, probs, oh1, v1, scr)
        if topk == 2:
            p.stt(p2[:], oh1[:], -1e9, probs[:], ALU.mult, ALU.add)
            _onehot_max(p, p2, oh2, v2, scr)
            p.tt(s12[:], v1[:], v2[:], ALU.add)
            p.recip(s12[:], s12[:])
            p.tt(v1[:], v1[:], s12[:], ALU.mult)
            p.tt(v2[:], v2[:], s12[:], ALU.mult)
        pairs = [(oh1, v1, 0)] + ([(oh2, v2, 1)] if topk == 2 else [])
        for (oh, vv, k) in pairs:
            p.vcopy(ohb[:], oh[:])
            p.mm(pc[0:128, 0:8], tri[:], ohb[:], True, True)
            p.tt(rk[:], pc[0:128, 0:8], oh[:], ALU.mult)
            p.red(r_[:], rk[:], "sum")
            bcast_row(p, runB, runoff[k], 8)
            p.tt(rk[:], runB[:, 0:8], oh[:], ALU.mult)
            p.red(off[:], rk[:], "sum")
            p.tt(r_[:], r_[:], off[:], ALU.add)
            p.dma(rvs[k][st, :], r_[:])
            p.dma(ohs[k][st, :], oh[:])
            p.dma(wvs[k][st, :], vv[:])
            p.mm(pt[0:1, 0:8], p.cst["onescb"][0:128, 0:1], ohb[:], True, True)
            p.tt(runoff[k][:], runoff[k][:], pt[0:1, 0:8], ALU.add)
    for k in range(topk):
        p.dma(coreto[k:k + 1, :], runoff[k][:])
    p.fin()
    return p


def launch_moe(topk, cap, func, passthrough):
    p = P()
    load_consts(p)
    xnTb = p.inp("xnTb", (D, T), BF16)
    xnTf = p.inp("xnTf", (D, T)) if passthrough else None
    xprevT = p.inp("xprevT", (D, T))
    sT = p.inp("sT", (D, T))
    W1s = [p.inp(f"W1_{e}", (D, 4096)) for e in range(8)]
    b1s = [p.inp(f"b1_{e}", (32, 128)) for e in range(8)]
    W2s = [p.inp(f"W2_{e}", (4096, D)) for e in range(8)]
    b2s = [p.inp(f"b2_{e}", (8, 128)) for e in range(8)]
    alltots = [p.inp(f"alltot{k}", (8, 8)) for k in range(topk)]
    cmask = p.inp("c_coremask", (8, 1))
    ohs = [p.inp(f"oh{k+1}", (T, 8)) for k in range(topk)]
    wvs = [p.inp(f"w{k+1}v", (T, 1)) for k in range(topk)] if topk == 2 else []
    rvs = [p.inp(f"r{k+1}v", (T, 1)) for k in range(topk)]
    xoutT = p.out("xoutT", (D, T))
    yacc = p.internal("yacc_i", (D, T))
    wT_hbm = p.internal("wT_i", (8, T))
    krow = p.sb((1, T), F32, "krow") if passthrough else None
    with p.scope():
        cm = p.sb((8, 1), F32, "cm")
        p.dma(cm[:], cmask)
        coreoffB = []
        for k in range(topk):
            at = p.sb((8, 8), F32, "at")
            p.dma(at[:], alltots[k])
            pco = p.ps((1, 8), F32, "pco")
            p.mm(pco[0:1, 0:8], cm[0:8, 0:1], at[0:8, 0:8], True, True)
            cb = p.sb((128, 8), F32, "cob")
            rowsb = p.sb((1, 8), F32, "cor")
            p.copy(rowsb[:], pco[0:1, 0:8])
            bcast_row(p, cb, rowsb, 8)
            coreoffB.append(cb)
        oh = p.sb((128, 8), F32, "oh")
        rv = p.sb((128, 1), F32, "rv")
        wv = p.sb((128, 1), F32, "wv")
        sel = p.sb((128, 8), F32, "sel")
        gr = p.sb((128, 1), F32, "gr")
        keep = p.sb((128, 1), F32, "kp")
        wacc = p.sb((128, 8), F32, "wacc")
        wrow = p.sb((8, 128), F32, "wrow")
        for tt in range(TT):
            st = slice(tt * 128, (tt + 1) * 128)
            p.memset(wacc[:], 0.0)
            for k in range(topk):
                p.dma(oh[:], ohs[k][st, :])
                p.dma(rv[:], rvs[k][st, :])
                p.tt(sel[:], oh[:], coreoffB[k][:], ALU.mult)
                p.red(gr[:], sel[:], "sum")
                p.tt(gr[:], gr[:], rv[:], ALU.add)
                p.ts(keep[:], gr[:], float(cap), ALU.is_le)
                if topk == 2:
                    p.dma(wv[:], wvs[k][st, :])
                    p.tt(keep[:], keep[:], wv[:], ALU.mult)
                p.ts(sel[:], oh[:], keep[:], ALU.mult)
                p.tt(wacc[:], wacc[:], sel[:], ALU.add)
            p.tr(p.ptf[0:8, 0:128], wacc[0:128, 0:8], p.cst["eyef"][:])
            p.copy(wrow[:], p.ptf[0:8, 0:128])
            p.dma(wT_hbm[0:8, st], wrow[:])
            if passthrough:
                p.red(keep[:], wacc[:], "sum")
                p.tr(p.ptf[0:1, 0:128], keep[0:128, 0:1], p.cst["eyef"][:])
                p.copy(krow[0:1, st], p.ptf[0:1, 0:128])
    for e in range(8):
        moe_expert_dense(p, W1s[e], W2s[e], b1s[e], b2s[e], xnTb, yacc,
                         wT_hbm, e, e == 0, func)
    with p.scope():
        a = p.sb((128, T), F32, "ca")
        g = p.sb((128, T), F32, "cg")
        kB = p.sb((128, 512), F32, "ckB") if passthrough else None
        for dt in range(DT):
            sd = slice(dt * 128, (dt + 1) * 128)
            p.dma(g[:], yacc[sd, :])
            if passthrough:
                p.dma(a[:], xnTf[sd, :])
                for tc in range(TC):
                    s = slice(tc * 512, (tc + 1) * 512)
                    bcast_row(p, kB, krow[0:1, s], 512)
                    p.ts(kB[:, 0:512], kB[:, 0:512], -1.0, ALU.mult, 1.0, ALU.add)
                    p.tt(kB[:, 0:512], kB[:, 0:512], a[:, s], ALU.mult)
                    p.tt(g[:, s], g[:, s], kB[:, 0:512], ALU.add)
            p.act(g[:], g[:], AF.Silu)
            p.dma(a[:], sT[sd, :])
            p.tt(g[:], g[:], a[:], ALU.mult)
            p.dma(a[:], xprevT[sd, :])
            p.tt(g[:], g[:], a[:], ALU.add)
            p.dma(xoutT[sd, :], g[:])
    p.fin()
    return p


def launch_l2attn():
    p = P()
    load_consts(p)
    xnT = p.inp("xnT", (D, T), BF16)
    Wkvd = p.inp("Wkvd", (D, 128))
    Wkup = p.inp("Wkup", (128, D))
    Wvup = p.inp("Wvup", (128, D))
    Wq = p.inp("Wq", (D, D))
    Wqr = p.inp("Wqr", (D, 32))
    Wkr = p.inp("Wkr", (128, 32))
    Wo = p.inp("Wo", (D, D))
    cosT = p.inp("c_cosT", (16, T))
    sinT = p.inp("c_sinT", (16, T))
    bm = p.inp("c_bm", (128, 1280), BF16)
    s2T = p.out("s2T", (D, T))
    qT = p.internal("qT_i", (D, T), BF16)
    kT = p.internal("kT_i", (D, T), BF16)
    vT = p.internal("vT_i", (D, T), BF16)
    vext = p.internal("vext_i", (T, 1040), BF16)
    ctxT = p.internal("ctxT_i", (D, T), BF16)
    bm_sb = p.sb((128, 1280), BF16, "bm")
    p.dma(bm_sb[:], bm)
    scale = 1.0 / np.sqrt(96.0)
    rope = {k: p.sb((16, T), BF16, k) for k in ["klo", "khi"]}
    rope["qlo_h"] = p.internal("qlo_i", (16, T), BF16)
    rope["qhi_h"] = p.internal("qhi_i", (16, T), BF16)
    with p.scope():
        xs = [p.sb((128, T), BF16, "xs") for _ in range(DT)]
        for dt in range(DT):
            p.dma(xs[dt][:], xnT[dt * 128:(dt + 1) * 128, :])
        with p.scope():
            wt = cast_load_lhsT(p, Wq, D, D, "w")
            proj_T(p, wt, xs, qT, D, func=AF.Copy, scale=scale)
        ckv = p.sb((128, T), BF16, "ckv")
        with p.scope():
            wt = cast_load_lhsT(p, Wkvd, D, 128, "wkv")
            pq = p.ps((128, 512), F32, "pq")
            for tc in range(TC):
                for ct in range(DT):
                    p.mm(pq[:], wt[ct][:, 0:128],
                         xs[ct][:, tc * 512:(tc + 1) * 512], ct == 0, ct == DT - 1)
                p.act(ckv[:, tc * 512:(tc + 1) * 512], pq[:], AF.Copy)
        with p.scope():
            for (W, outT) in [(Wkup, kT), (Wvup, vT)]:
                wt = cast_load_lhsT(p, W, 128, D, "wu")
                proj_T(p, wt, [ckv], outT, D, func=AF.Copy, kin=1)
        with p.scope():
            coss = p.sb((16, T), F32, "coss")
            sins = p.sb((16, T), F32, "sins")
            p.dma(coss[:], cosT)
            p.dma(sins[:], sinT)
            lo = p.sb((16, T), F32, "lo")
            hi = p.sb((16, T), F32, "hi")
            rl = p.sb((16, T), F32, "rl")
            rh = p.sb((16, T), F32, "rh")
            pq = p.ps((128, 512), F32, "pq2")
            for (W, kin, src, qside) in [(Wqr, 8, xs, True), (Wkr, 1, [ckv], False)]:
                wt = cast_load_lhsT(p, W, 128 * kin, 32, "wr")
                for (c0, dst) in [(0, lo), (16, hi)]:
                    for tc in range(TC):
                        for ct in range(kin):
                            p.mm(pq[0:16, 0:512], wt[ct][:, c0:c0 + 16],
                                 src[ct][:, tc * 512:(tc + 1) * 512],
                                 ct == 0, ct == kin - 1)
                        p.copy(dst[0:16, tc * 512:(tc + 1) * 512], pq[0:16, 0:512])
                p.tt(rl[:], lo[:], coss[:], ALU.mult)
                p.tt(rh[:], hi[:], sins[:], ALU.mult)
                p.tt(rl[:], rl[:], rh[:], ALU.subtract)
                p.tt(rh[:], hi[:], coss[:], ALU.mult)
                p.tt(lo[:], lo[:], sins[:], ALU.mult)
                p.tt(rh[:], rh[:], lo[:], ALU.add)
                if qside:
                    qb = p.sb((16, T), BF16, "qb")
                    p.act(qb[:], rl[:], AF.Copy, scale=scale)
                    p.dma(rope["qlo_h"][0:16, :], qb[:])
                    p.act(qb[:], rh[:], AF.Copy, scale=scale)
                    p.dma(rope["qhi_h"][0:16, :], qb[:])
                else:
                    p.act(rope["klo"][:], rl[:], AF.Copy)
                    p.act(rope["khi"][:], rh[:], AF.Copy)
    build_vext_from_vT(p, vT, vext, 16)
    attn_16head(p, qT, kT, vext, ctxT, causal=True, bm_sb=bm_sb, rope=rope)
    with p.scope():
        xs = [p.sb((128, T), BF16, "xs2") for _ in range(DT)]
        for dt in range(DT):
            p.dma(xs[dt][:], ctxT[dt * 128:(dt + 1) * 128, :])
        wt = cast_load_lhsT(p, Wo, D, D, "wo")
        proj_T(p, wt, xs, None, D, func=AF.Copy, out_hbm_f32=s2T)
    p.fin()
    return p


def launch_l3():
    p = P()
    load_consts(p)
    x3T = p.inp("x3T", (D, T))
    Wz = p.inp("Wz", (D, D)); bz = p.inp("bz", (8, 128))
    Wh = p.inp("Wh", (D, D)); bh = p.inp("bh", (8, 128))
    w1 = p.inp("w1", (D, 4096))
    b1 = p.inp("b1", (32, 128))
    w2 = p.inp("w2", (4096, D))
    b2 = p.inp("b2", (8, 128))
    xout = p.out("xout", (T, D))
    xnT = p.internal("xnT_i", (D, T), BF16)
    sT = p.internal("sT_i", (D, T))
    yacc = p.internal("yacc_i", (D, T))
    ln_T(p, x3T, xnT)
    with p.scope():
        xs = [p.sb((128, T), BF16, "xs") for _ in range(DT)]
        for dt in range(DT):
            p.dma(xs[dt][:], xnT[dt * 128:(dt + 1) * 128, :])
        zt = p.sb((128, T), F32, "zt")
        htl = p.sb((128, T), F32, "htl")
        at = p.sb((128, T), F32, "at")
        bt_ = p.sb((128, T), F32, "bt")
        hout = p.sb((128, T), F32, "hout")
        wtz = cast_load_lhsT(p, Wz, D, D, "wz")
        wth = cast_load_lhsT(p, Wh, D, D, "wh")
        bzt = load_bias_tile(p, bz, 8)
        bht = load_bias_tile(p, bh, 8)
        pq = p.ps((128, 512), F32, "pq")
        for mt in range(DT):
            for tc in range(TC):
                s = slice(tc * 512, (tc + 1) * 512)
                for ct in range(DT):
                    p.mm(pq[:], wtz[ct][:, mt * 128:(mt + 1) * 128], xs[ct][:, s],
                         ct == 0, ct == DT - 1)
                p.act(zt[:, s], pq[:], AF.Sigmoid, bias=bzt[:, mt:mt + 1])
                for ct in range(DT):
                    p.mm(pq[:], wth[ct][:, mt * 128:(mt + 1) * 128], xs[ct][:, s],
                         ct == 0, ct == DT - 1)
                p.ts(htl[:, s], pq[:], bht[:, mt:mt + 1], ALU.add)
            p.ts(at[:], zt[:], -1.0, ALU.mult, 1.0, ALU.add)
            p.tt(bt_[:], zt[:], htl[:], ALU.mult)
            p.scan(hout[:], at[:], bt_[:])
            p.dma(sT[mt * 128:(mt + 1) * 128, :], hout[:])
    moe_expert_dense(p, w1, w2, b1, b2, xnT, yacc, None, 0, True, AF.Relu)
    with p.scope():
        a = p.sb((128, T), F32, "ca")
        g = p.sb((128, T), F32, "cg")
        cf = p.sb((128, 128), F32, "cf")
        for dt in range(DT):
            sd = slice(dt * 128, (dt + 1) * 128)
            p.dma(g[:], yacc[sd, :])
            p.act(g[:], g[:], AF.Silu)
            p.dma(a[:], sT[sd, :])
            p.tt(g[:], g[:], a[:], ALU.mult)
            p.dma(a[:], x3T[sd, :])
            p.tt(g[:], g[:], a[:], ALU.add)
            for tt in range(TT):
                p.tr(p.ptf[:], g[:, tt * 128:(tt + 1) * 128], p.cst["eyef"][:])
                p.copy(cf[:], p.ptf[:])
                p.dma(xout[tt * 128:(tt + 1) * 128, sd], cf[:])
    p.fin()
    return p


def _r(x):
    return np.ascontiguousarray(x)


def _run(pbuild, in_maps):
    p = pbuild
    p.close()
    res = run_bass_kernel_spmd(p.nc, in_maps, core_ids=list(range(NCORE)))
    return res.results


def _bias2d(b):
    return _r(np.asarray(b, np.float32).reshape(8, 128))


def _bias2d32(b):
    return _r(np.asarray(b, np.float32).reshape(32, 128))


def kernel(_dbg=None, **I):
    HC = host_consts()
    bmv = np.zeros((128, 1280), np.float32)
    for pp in range(128):
        bmv[pp, :] = (pp <= np.arange(1280) - 640)
    HC2 = dict(HC)
    HC2["c_bm"] = bmv.astype(bfdt)
    iotaB = np.ascontiguousarray(np.broadcast_to(np.arange(256, dtype=np.float32), (128, 256)))
    cvh = ((np.arange(32)[None, :] * 128 + np.arange(128)[:, None]) // 16).astype(np.float32)
    cvl = (np.arange(2)[None, :] * 128 + np.arange(128)[:, None]).astype(np.float32)
    ct0 = (np.arange(128)[:, None] + 128 * np.arange(32)[None, :] - 4095).astype(np.float32)
    xx = (2.0 ** 8) ** (1.0 / 16)
    slopes = np.asarray([[1.0 / xx ** (i + 1) for i in range(16)]], np.float32)
    tri128 = (np.arange(128)[:, None] <= np.arange(128)[None, :]).astype(bfdt)
    inv_freq = 1.0 / 10000.0 ** (np.arange(0, 32, 2, dtype=np.float32) / 32)
    tpos = np.arange(T, dtype=np.float32)
    freqs = np.outer(tpos, inv_freq)
    cosT = _r(np.cos(freqs).T.astype(np.float32))
    sinT = _r(np.sin(freqs).T.astype(np.float32))

    x_all = np.asarray(I["L0_attn_x"], np.float32)
    shards = [np.ascontiguousarray(x_all[c]) for c in range(NCORE)]

    def rep(d):
        return [d for _ in range(NCORE)]

    def run(p, per_core_extra, shared):
        maps = []
        for c in range(NCORE):
            m = dict(HC)
            m.update(shared)
            m.update(per_core_extra[c])
            maps.append(m)
        return _run(p, maps)

    r = run(launch_prep(), [{"x": shards[c]} for c in range(NCORE)], {})
    xT = [r[c]["xT"] for c in range(NCORE)]
    xn0T = [r[c]["xn0T"] for c in range(NCORE)]
    xnfT = [r[c]["xnfT"] for c in range(NCORE)]
    xnfTb = [r[c]["xnfTb"] for c in range(NCORE)]
    if _dbg is not None: _dbg.update(xT=xT, xn0T=xn0T, xnfT=xnfT)

    sh = {"Wq": _r(I["L0_attn_Wq"]), "bq": _bias2d(I["L0_attn_bq"]),
          "Wk": _r(I["L0_attn_Wk"]), "bk": _bias2d(I["L0_attn_bk"]),
          "Wv": _r(I["L0_attn_Wv"]), "bv": _bias2d(I["L0_attn_bv"]),
          "Wo": _r(I["L0_attn_Wo"]), "bo": _bias2d(I["L0_attn_bo"]),
          "c_bm": HC2["c_bm"]}
    r = run(launch_l0attn(), [{"xn0T": xn0T[c]} for c in range(NCORE)], sh)
    s0T = [r[c]["s0T"] for c in range(NCORE)]
    if _dbg is not None: _dbg.update(s0T=s0T)

    ffts = {}
    fftf = {}
    for w in ["q", "k", "v"]:
        sh = {"nwT": _r(np.pad(np.asarray(I[f"L0_ffn_{w}_nw"], np.float32).T, ((0, 0), (0, 1)))),
              "nb": _r(np.asarray(I[f"L0_ffn_{w}_nb"], np.float32).reshape(1, 255)),
              "W1c": _r(np.asarray(I[f"L0_ffn_{w}_w1"], np.float32)
                        .transpose(1, 0, 2).reshape(1024, 4096)),
              "b1c": _bias2d32(np.asarray(I[f"L0_ffn_{w}_b1"], np.float32).reshape(-1)),
              "W2c": _r(np.asarray(I[f"L0_ffn_{w}_w2"], np.float32).reshape(4096, 1024)),
              "b2": _r(np.asarray(I[f"L0_ffn_{w}_b2"], np.float32)),
              "c_iotaB": iotaB, "c_cvh": cvh, "c_cvl": cvl}
        r = run(launch_fff_qkv(w), [{"xnfTb": xnfTb[c], "xnfTf": xnfT[c]}
                                    for c in range(NCORE)], sh)
        ffts[w] = [r[c]["fT"] for c in range(NCORE)]

    r = run(launch_l0ffn_attn(),
            [{"qT": ffts["q"][c], "kT": ffts["k"][c], "vT": ffts["v"][c],
              "xnfT": xnfT[c]} for c in range(NCORE)], {})
    yT = [r[c]["yT"] for c in range(NCORE)]
    yTb = [r[c]["yTb"] for c in range(NCORE)]
    if _dbg is not None: _dbg.update(yT=yT, ffts=ffts)

    sh = {"nwT": _r(np.pad(np.asarray(I["L0_ffn_f_nw"], np.float32).T, ((0, 0), (0, 1)))),
          "nb": _r(np.asarray(I["L0_ffn_f_nb"], np.float32).reshape(1, 255)),
          "W1c": _r(np.asarray(I["L0_ffn_f_w1"], np.float32)
                    .transpose(1, 0, 2).reshape(1024, 4096)),
          "b1c": _bias2d32(np.asarray(I["L0_ffn_f_b1"], np.float32).reshape(-1)),
          "W2c": _r(np.asarray(I["L0_ffn_f_w2"], np.float32).reshape(4096, 1024)),
          "b2": _r(np.asarray(I["L0_ffn_f_b2"], np.float32)),
          "c_iotaB": iotaB, "c_cvh": cvh, "c_cvl": cvl}
    r = run(launch_l0ffn_final(),
            [{"yTb": yTb[c], "yT": yT[c], "xT": xT[c], "s0T": s0T[c]}
             for c in range(NCORE)], sh)
    x1T = [r[c]["x1T"] for c in range(NCORE)]
    if _dbg is not None: _dbg.update(x1T=x1T)

    r = run(launch_ln(True), [{"lnin": x1T[c]} for c in range(NCORE)], {})
    xn1T = [r[c]["lnout"] for c in range(NCORE)]
    xn1Tf = [r[c]["lnoutf"] for c in range(NCORE)]

    sh = {"Wq": _r(I["L1_attn_Wq"]), "bq": _bias2d(I["L1_attn_bq"]),
          "Wk": _r(I["L1_attn_Wk"]), "bk": _bias2d(I["L1_attn_bk"]),
          "Wv": _r(I["L1_attn_Wv"]), "bv": _bias2d(I["L1_attn_bv"]),
          "Wo": _r(I["L1_attn_Wo"]), "bo": _bias2d(I["L1_attn_bo"]),
          "c_ct0": ct0, "c_slopes": slopes}
    r = run(launch_l1attn(), [{"xnT": xn1T[c]} for c in range(NCORE)], sh)
    s1T = [r[c]["s1T"] for c in range(NCORE)]
    if _dbg is not None: _dbg.update(s1T=s1T, xn1T=xn1T)

    sh = {"Wr": _r(I["L1_ffn_Wr"]), "c_tri128": tri128}
    r = run(launch_route(2, False), [{"xnTf": xn1Tf[c]} for c in range(NCORE)], sh)
    route1 = r
    alltot1 = [np.ascontiguousarray(
        np.stack([np.asarray(route1[c]["coretot"][k], np.float32)
                  for c in range(NCORE)], 0)) for k in range(2)]

    sh = {}
    for e in range(8):
        sh[f"W1_{e}"] = _r(np.asarray(I["L1_ffn_W1"][e], np.float32))
        sh[f"b1_{e}"] = _bias2d32(np.asarray(I["L1_ffn_b1"][e], np.float32))
        sh[f"W2_{e}"] = _r(np.asarray(I["L1_ffn_W2"][e], np.float32))
        sh[f"b2_{e}"] = _bias2d(np.asarray(I["L1_ffn_b2"][e], np.float32))
    per = []
    for c in range(NCORE):
        m = {"xnTb": xn1T[c], "xprevT": x1T[c], "sT": s1T[c],
             "alltot0": alltot1[0], "alltot1": alltot1[1],
             "c_coremask": (np.arange(8)[:, None] < c).astype(np.float32)}
        for k in range(2):
            m[f"oh{k+1}"] = route1[c][f"oh{k+1}"]
            m[f"w{k+1}v"] = route1[c][f"w{k+1}v"]
            m[f"r{k+1}v"] = route1[c][f"r{k+1}v"]
        per.append(m)
    r = run(launch_moe(2, 5120, AF.Gelu, False), per, sh)
    x2T = [r[c]["xoutT"] for c in range(NCORE)]
    if _dbg is not None: _dbg.update(x2T=x2T, route1=route1, alltot1=alltot1)

    r = run(launch_ln(True), [{"lnin": x2T[c]} for c in range(NCORE)], {})
    xn2T = [r[c]["lnout"] for c in range(NCORE)]
    xn2Tf = [r[c]["lnoutf"] for c in range(NCORE)]

    sh = {"Wkvd": _r(I["L2_attn_W_kv_down"]), "Wkup": _r(I["L2_attn_W_k_up"]),
          "Wvup": _r(I["L2_attn_W_v_up"]), "Wq": _r(I["L2_attn_W_q"]),
          "Wqr": _r(I["L2_attn_W_q_rope"]), "Wkr": _r(I["L2_attn_W_k_rope"]),
          "Wo": _r(I["L2_attn_W_o"]), "c_cosT": cosT, "c_sinT": sinT,
          "c_bm": HC2["c_bm"]}
    r = run(launch_l2attn(), [{"xnT": xn2T[c]} for c in range(NCORE)], sh)
    s2T = [r[c]["s2T"] for c in range(NCORE)]
    if _dbg is not None: _dbg.update(s2T=s2T, xn2T=xn2T)

    sh = {"Wr": _r(I["L2_ffn_w_switch"]), "c_tri128": tri128,
          "brsw": _r(np.asarray(I["L2_ffn_b_switch"], np.float32).reshape(1, 8))}
    r = run(launch_route(1, True), [{"xnTf": xn2Tf[c]} for c in range(NCORE)], sh)
    route2 = r
    alltot2 = np.ascontiguousarray(
        np.stack([np.asarray(route2[c]["coretot"][0], np.float32)
                  for c in range(NCORE)], 0))

    sh = {}
    for e in range(8):
        sh[f"W1_{e}"] = _r(np.asarray(I["L2_ffn_w1"][e], np.float32))
        sh[f"b1_{e}"] = _bias2d32(np.asarray(I["L2_ffn_b1"][e], np.float32))
        sh[f"W2_{e}"] = _r(np.asarray(I["L2_ffn_w2"][e], np.float32))
        sh[f"b2_{e}"] = _bias2d(np.asarray(I["L2_ffn_b2"][e], np.float32))
    per = []
    for c in range(NCORE):
        m = {"xnTb": xn2T[c], "xnTf": xn2Tf[c], "xprevT": x2T[c], "sT": s2T[c],
             "alltot0": alltot2,
             "c_coremask": (np.arange(8)[:, None] < c).astype(np.float32),
             "oh1": route2[c]["oh1"], "r1v": route2[c]["r1v"]}
        per.append(m)
    r = run(launch_moe(1, 2048, AF.Relu, True), per, sh)
    x3T = [r[c]["xoutT"] for c in range(NCORE)]
    if _dbg is not None: _dbg.update(x3T=x3T, route2=route2)

    sh = {"Wz": _r(I["L3_attn_Wz"]), "bz": _bias2d(I["L3_attn_bz"]),
          "Wh": _r(I["L3_attn_Wh"]), "bh": _bias2d(I["L3_attn_bh"]),
          "w1": _r(I["L3_ffn_w1"]), "b1": _bias2d32(I["L3_ffn_b1"]),
          "w2": _r(I["L3_ffn_w2"]), "b2": _bias2d(I["L3_ffn_b2"])}
    r = run(launch_l3(), [{"x3T": x3T[c]} for c in range(NCORE)], sh)
    out = np.stack([np.asarray(r[c]["xout"], np.float32) for c in range(NCORE)], 0)
    return out
